# version 02 (final)
# baseline (speedup 1.0000x reference)
import numpy as np
from contextlib import ExitStack

import concourse.bass as bass
import concourse.mybir as mybir
from concourse.bass_utils import run_bass_kernel_spmd

F32 = mybir.dt.float32
AF = mybir.ActivationFunctionType
ALU = mybir.AluOpType

B, L, D = 4, 4096, 1024
DIN, DST, DTR = 2048, 16, 64
NE, HD = 15, 1024
KD = D // 128
KI = DIN // 128
KH = HD // 128
T = 256
NCH = L // T
TF = 512
NTT = 1024 // TF
EPS = 1e-5
DEBUG_TAPS = False


class Ctr:
    def __init__(self, sem):
        self.sem = sem
        self.n = 0

    def inc(self, inst, k=1):
        inst.then_inc(self.sem, k)
        self.n += k


def build_nc():
    nc = bass.Bass("TRN2", target_bir_lowering=False, debug=False)

    din = {}
    def IN(name, shape):
        din[name] = nc.dram_tensor(name, list(shape), F32, kind="ExternalInput").ap()
        return din[name]

    xT = IN("xT", (D, L))
    w_in = IN("w_in", (D, 2 * DIN))
    w_x = IN("w_x", (DIN, 96))
    w_dt = IN("w_dt", (DTR, DIN))
    w_out = IN("w_out", (DIN, D))
    conv_wp = IN("conv_wp", (128, KI * 4))
    conv_bp = IN("conv_bp", (128, KI))
    dtb_p = IN("dtb_p", (128, KI))
    D_pp = IN("D_p", (128, KI))
    Alog_p = IN("Alog_p", (128, KI * DST))
    nw_p = IN("nw_p", (128, KD))
    w_s1 = IN("w_s1", (D, HD))
    w_s2 = IN("w_s2", (HD, D))
    sb1_p = IN("sb1_p", (128, KH))
    sb2_p = IN("sb2_p", (128, KD))
    w_r = IN("w_r", (D, NE))
    rb_p = IN("rb_p", (NE, 1))
    w_e1 = IN("w_e1", (NE, D, HD))
    w_e2 = IN("w_e2", (NE, HD, D))
    eb1_p = IN("eb1_p", (128, NE * KH))
    eb2_n = IN("eb2_n", (NE, D))
    onesc = IN("onesc", (128, 1))
    onesr = IN("onesr", (1, 128))
    ident = IN("ident", (128, 128))
    selBC = IN("selBC", (32, 32 * 128))
    sel15 = IN("sel15", (NE, NE * 128))
    svec = IN("svec", (128, 2))

    out = nc.dram_tensor("out", [D, 2048], F32, kind="ExternalOutput").ap()
    x1buf = nc.dram_tensor("x1buf", [D, L], F32, kind="Internal").ap()
    xselbuf = nc.dram_tensor("xselbuf", [D, 2048], F32, kind="Internal").ap()

    ctx = ExitStack()
    SB = lambda name, shape, dt=F32: ctx.enter_context(nc.sbuf_tensor(name, list(shape), dt))
    SEM = lambda name: ctx.enter_context(nc.semaphore(name))

    c_onesc = SB("c_onesc", (128, 1))
    c_onesr = SB("c_onesr", (1, 128))
    c_ident = SB("c_ident", (128, 128))
    c_convw = SB("c_convw", (128, KI * 4))
    c_convb = SB("c_convb", (128, KI))
    c_dtb = SB("c_dtb", (128, KI))
    c_D = SB("c_D", (128, KI))
    c_A = SB("c_A", (128, KI * DST))
    c_nw = SB("c_nw", (128, KD))
    c_sb1 = SB("c_sb1", (128, KH))
    c_sb2 = SB("c_sb2", (128, KD))
    c_rb = SB("c_rb", (NE, 1))
    c_eb1 = SB("c_eb1", (128, NE * KH))
    c_svec = SB("c_svec", (128, 2))

    sDC = Ctr(SEM("sDC"))
    sDX = Ctr(SEM("sDX"))
    sD0 = Ctr(SEM("sD0"))
    sD1 = Ctr(SEM("sD1"))
    sDW = Ctr(SEM("sDW"))
    sDF = Ctr(SEM("sDF"))
    sDL = Ctr(SEM("sDL"))
    sG = Ctr(SEM("sG"))
    sT = Ctr(SEM("sT"))
    sV = Ctr(SEM("sV"))
    sA = Ctr(SEM("sA"))
    sP = Ctr(SEM("sP"))

    blk = nc.Block()
    block = blk.__enter__()

    mctx = ExitStack()
    MSB = lambda name, shape, dt=F32: mctx.enter_context(nc.sbuf_tensor(name, list(shape), dt))
    MPS = lambda name, shape: mctx.enter_context(nc.psum_tensor(name, list(shape), F32))

    c_selBC = MSB("c_selBC", (32, 32 * 128))
    w_xs = MSB("w_xs", (128, KI * 96))
    w_dts = MSB("w_dts", (DTR, DIN))
    sstate = MSB("sstate", (128, KI * DST))
    bnd = MSB("bnd", (128, KI * 3))
    xb = MSB("xb", (128, KD * T))
    t1b = MSB("t1b", (128, KD * T))
    hb = MSB("hb", (128, KD * T))
    ub = MSB("ub", (128, KI * T))
    sresb = MSB("sresb", (128, KI * T))
    deltab = MSB("deltab", (128, KI * T))
    dub = MSB("dub", (128, KI * T))
    yb = MSB("yb", (128, KI * T))
    ab = MSB("ab", (128, KD * T))
    sqb = MSB("sqb", (128, KD * T))
    xs2 = MSB("xs2", (128, 2 * (T + 3)))
    drb = MSB("drb", (DTR, T))
    bcsb = MSB("bcsb", (32, T))
    cbs = MSB("cbs", (128, 2 * T))
    zb = MSB("zb", (128, 2 * T))
    dAb = MSB("dAb", (128, 2 * T))
    dBb = MSB("dBb", (128, 3 * T))
    gt = MSB("gt", (128, T))
    gt2 = MSB("gt2", (128, T))
    gty = MSB("gty", (128, 2 * T))
    mbs = MSB("mbs", (128, T))
    rs1s = MSB("rs1s", (128, T))
    rs2s = MSB("rs2s", (128, T))
    rsas = MSB("rsas", (128, T))
    r0s = MSB("r0s", (128, T))
    rowm = MSB("rowm", (1, T))
    rowv = MSB("rowv", (1, T))
    rowa = MSB("rowa", (1, T))
    rowb = MSB("rowb", (1, T))
    rowc = MSB("rowc", (1, T))
    rowd = MSB("rowd", (1, T))
    rowe = MSB("rowe", (1, T))
    wslab = MSB("wslab", (128, 2 * 512))
    p = [MPS(f"p{i}", (128, T)) for i in range(8)]

    @block.sync
    def _(q):
        for dst, src in ((c_onesc, onesc), (c_onesr, onesr), (c_ident, ident),
                         (c_selBC, selBC), (c_convw, conv_wp),
                         (c_convb, conv_bp), (c_dtb, dtb_p), (c_D, D_pp),
                         (c_A, Alog_p), (c_nw, nw_p), (c_sb1, sb1_p),
                         (c_sb2, sb2_p), (c_rb, rb_p), (c_eb1, eb1_p),
                         (c_svec, svec)):
            sDC.inc(q.dma_start(out=dst[:], in_=src), 16)
        sDC.inc(q.dma_start(out=w_xs[:].rearrange("p (kc m) -> p kc m", m=96),
                            in_=w_x.rearrange("(kc p) m -> p kc m", p=128)), 16)
        sDC.inc(q.dma_start(out=w_dts[:], in_=w_dt), 16)

    for _name in ("tensor", "gpsimd"):
        def _wconst(q):
            q.wait_ge(sDC.sem, sDC.n)
        getattr(block, _name)(_wconst)

    @block.scalar
    def _(q):
        q.wait_ge(sDC.sem, sDC.n)
        sA.inc(q.activation(c_A[:], c_A[:], AF.Exp))

    @block.vector
    def _(q):
        q.wait_ge(sA.sem, sA.n)
        q.tensor_scalar_mul(c_A[:], c_A[:], -1.0)
        q.memset(sstate[:], 0.0)
        sV.inc(q.memset(bnd[:], 0.0))

    for ch in range(NCH):
        t0 = ch * T

        @block.sync
        def _(q, t0=t0):
            q.wait_ge(sV.sem, sV.n)
            q.wait_ge(sP.sem, sP.n)
            sDX.inc(q.dma_start(
                out=xb[:].rearrange("p (kc t) -> p kc t", t=T),
                in_=xT.rearrange("(kc p) t -> p kc t", p=128)[:, :, t0:t0 + T]), 16)

        @block.vector
        def _(q):
            q.wait_ge(sDX.sem, sDX.n)
            sV.inc(q.tensor_mul(sqb[:], xb[:], xb[:]))

        @block.tensor
        def _(q):
            q.wait_ge(sV.sem, sV.n)
            q.wait_ge(sA.sem, sA.n)
            for k in range(KD):
                q.matmul(p[0][0:1, :], c_onesc[:], xb[:, k * T:(k + 1) * T],
                         start=(k == 0), stop=(k == KD - 1))
            for k in range(KD):
                inst = q.matmul(p[1][0:1, :], c_onesc[:], sqb[:, k * T:(k + 1) * T],
                                start=(k == 0), stop=(k == KD - 1))
            sT.inc(inst)

        @block.scalar
        def _(q):
            q.wait_ge(sT.sem, sT.n)
            q.activation(rowm[:], p[0][0:1, :], AF.Copy, scale=1.0 / D)
            q.activation(rowa[:], p[1][0:1, :], AF.Copy, scale=1.0 / D)
            q.drain()
            sA.inc(q.activation(rowb[:], rowm[:], AF.Square))

        @block.vector
        def _(q):
            q.wait_ge(sA.sem, sA.n)
            q.tensor_sub(rowv[:], rowa[:], rowb[:])
            q.drain()
            q.tensor_scalar(rowd[:], rowv[:], 1.0 + EPS, EPS * EPS, ALU.mult, ALU.add)
            q.tensor_scalar_add(rowe[:], rowv[:], EPS)
            q.drain()
            q.reciprocal(rowa[:], rowd[:])
            sV.inc(q.reciprocal(rowb[:], rowe[:]))

        @block.scalar
        def _(q):
            q.wait_ge(sV.sem, sV.n)
            q.activation(rowa[:], rowa[:], AF.Sqrt)
            sA.inc(q.activation(rowb[:], rowb[:], AF.Sqrt))

        @block.tensor
        def _(q):
            q.wait_ge(sA.sem, sA.n)
            q.matmul(p[2][:], c_onesr[:], rowm[:], start=True, stop=True)
            q.matmul(p[3][:], c_onesr[:], rowa[:], start=True, stop=True)
            sT.inc(q.matmul(p[4][:], c_onesr[:], rowb[:], start=True, stop=True))

        @block.scalar
        def _(q):
            q.wait_ge(sT.sem, sT.n)
            q.copy(mbs[:], p[2][:])
            q.copy(rs2s[:], p[3][:])
            sA.inc(q.copy(rs1s[:], p[4][:]))

        @block.vector
        def _(q):
            q.wait_ge(sA.sem, sA.n)
            for k in range(KD):
                q.tensor_sub(t1b[:, k * T:(k + 1) * T], xb[:, k * T:(k + 1) * T], mbs[:])
            q.drain()
            for k in range(KD):
                inst = q.scalar_tensor_tensor(hb[:, k * T:(k + 1) * T],
                                              t1b[:, k * T:(k + 1) * T],
                                              c_nw[:, k:k + 1], rs2s[:],
                                              ALU.mult, ALU.mult)
            sV.inc(inst)

        for g in range(8):
            ticks = {}
            base_tick = sT.n
            for k in range(KD):
                buf = k % 2

                sDb = sD0 if buf == 0 else sD1

                @block.sync
                def _(q, g=g, k=k, buf=buf, sDb=sDb, guard=(ticks.get(k - 2, base_tick) if k >= 2 else base_tick)):
                    q.wait_ge(sT.sem, guard)
                    sDb.inc(q.dma_start(
                        out=wslab[:, buf * 512:(buf + 1) * 512],
                        in_=w_in[k * 128:(k + 1) * 128, g * 512:(g + 1) * 512]), 16)

                dmadone = sDb.n

                @block.tensor
                def _(q, g=g, k=k, buf=buf, sDb=sDb, dmadone=dmadone):
                    q.wait_ge(sDb.sem, dmadone)
                    if k == 0:
                        q.wait_ge(sV.sem, sV.n)
                        q.wait_ge(sA.sem, sA.n)
                    for ml in range(4):
                        inst = q.matmul(p[4 + ml][:],
                                        wslab[:, buf * 512 + ml * 128: buf * 512 + (ml + 1) * 128],
                                        hb[:, k * T:(k + 1) * T],
                                        start=(k == 0), stop=(k == KD - 1))
                    sT.inc(inst)

                ticks[k] = sT.n

            for ml in range(4):
                m = g * 4 + ml
                if m < 16:
                    dc = m
                    r = dc % 2
                    x_lo = r * (T + 3)

                    @block.scalar
                    def _(q, ml=ml, dc=dc, x_lo=x_lo):
                        q.wait_ge(sT.sem, sT.n)
                        q.wait_ge(sV.sem, sV.n)
                        q.wait_ge(sP.sem, sP.n)
                        q.copy(xs2[:, x_lo + 3: x_lo + 3 + T], p[4 + ml][:])
                        sA.inc(q.copy(xs2[:, x_lo:x_lo + 3], bnd[:, dc * 3:(dc + 1) * 3]))

                    @block.vector
                    def _(q, dc=dc, x_lo=x_lo):
                        q.wait_ge(sA.sem, sA.n)
                        q.drain()
                        q.tensor_scalar_mul(gt[:], xs2[:, x_lo:x_lo + T],
                                            c_convw[:, dc * 4:dc * 4 + 1])
                        q.tensor_scalar_mul(gt2[:], xs2[:, x_lo + 1:x_lo + 1 + T],
                                            c_convw[:, dc * 4 + 1:dc * 4 + 2])
                        q.drain()
                        q.scalar_tensor_tensor(gt[:], xs2[:, x_lo + 2:x_lo + 2 + T],
                                               c_convw[:, dc * 4 + 2:dc * 4 + 3],
                                               gt[:], ALU.mult, ALU.add)
                        q.scalar_tensor_tensor(gt2[:], xs2[:, x_lo + 3:x_lo + 3 + T],
                                               c_convw[:, dc * 4 + 3:dc * 4 + 4],
                                               gt2[:], ALU.mult, ALU.add)
                        q.drain()
                        sV.inc(q.tensor_add(gt[:], gt[:], gt2[:]))

                    @block.gpsimd
                    def _(q, dc=dc, x_lo=x_lo):
                        q.wait_ge(sV.sem, sV.n)
                        sP.inc(q.tensor_copy(bnd[:, dc * 3:(dc + 1) * 3],
                                             xs2[:, x_lo + T:x_lo + T + 3]))

                    @block.scalar
                    def _(q, dc=dc):
                        q.wait_ge(sV.sem, sV.n)
                        sA.inc(q.activation(ub[:, dc * T:(dc + 1) * T], gt[:], AF.Silu,
                                            bias=c_convb[:, dc:dc + 1]))
                else:
                    dc = m - 16

                    @block.scalar
                    def _(q, ml=ml, dc=dc):
                        q.wait_ge(sT.sem, sT.n)
                        sA.inc(q.activation(sresb[:, dc * T:(dc + 1) * T],
                                            p[4 + ml][:], AF.Silu))

        @block.tensor
        def _(q):
            q.wait_ge(sA.sem, sA.n)
            for k in range(KI):
                inst = q.matmul(p[0][0:96, :], w_xs[:, k * 96:(k + 1) * 96],
                                ub[:, k * T:(k + 1) * T],
                                start=(k == 0), stop=(k == KI - 1))
            sT.inc(inst)

        @block.scalar
        def _(q):
            q.wait_ge(sT.sem, sT.n)
            q.copy(drb[:], p[0][0:DTR, :])
            sA.inc(q.copy(bcsb[:], p[0][DTR:96, :]))

        for dc in range(KI):
            pm = p[1 + (dc % 2)]

            @block.tensor
            def _(q, dc=dc, pm=pm):
                q.wait_ge(sA.sem, sA.n)
                q.wait_ge(sV.sem, sV.n)
                sT.inc(q.matmul(pm[:], w_dts[:, dc * 128:(dc + 1) * 128], drb[:],
                                start=True, stop=True))

            @block.scalar
            def _(q, dc=dc, pm=pm):
                q.wait_ge(sT.sem, sT.n)
                q.drain()
                q.activation(gt2[:], pm[:], AF.Abs, bias=c_dtb[:, dc:dc + 1])
                q.drain()
                q.activation(gt2[:], gt2[:], AF.Exp, scale=-1.0)
                q.drain()
                sA.inc(q.activation(gt2[:], gt2[:], AF.Ln, bias=1.0))

            @block.vector
            def _(q, dc=dc, pm=pm):
                q.wait_ge(sA.sem, sA.n)
                q.tensor_scalar(dAb[:, 0:T], pm[:], c_dtb[:, dc:dc + 1], 0.0,
                                ALU.add, ALU.max)
                q.drain()
                q.tensor_add(deltab[:, dc * T:(dc + 1) * T], dAb[:, 0:T], gt2[:])
                q.drain()
                sV.inc(q.tensor_mul(dub[:, dc * T:(dc + 1) * T],
                                    deltab[:, dc * T:(dc + 1) * T],
                                    ub[:, dc * T:(dc + 1) * T]))

        NIT = DST * KI
        dbu_tick = {}
        scan_tick = {}
        dA_tick = {}
        ymul_tick = {}
        yadd_tick = {}
        bc_tick = {}
        cbs_tick = {}
        v_base = sV.n
        gp_base = sP.n
        a_base = sA.n

        @block.gpsimd
        def _(q):
            q.wait_ge(sV.sem, sV.n)
            sP.inc(q.memset(yb[:], 0.0))

        @block.vector
        def _(q):
            q.drain()

        def emit_bc(n):
            pB = p[5 + (n % 2)]

            @block.tensor
            def _(q, n=n,
                  vg=dbu_tick.get((n - 2) * KI + KI - 1, v_base),
                  pg=ymul_tick.get((n - 2) * KI + KI - 1, gp_base),
                  ag=cbs_tick.get(n - 1, a_base),
                  pB=pB):
                q.wait_ge(sV.sem, vg)
                q.wait_ge(sP.sem, pg)
                q.wait_ge(sA.sem, ag)
                q.matmul(pB[:], c_selBC[:, n * 128:(n + 1) * 128], bcsb[:],
                         start=True, stop=True)
                sT.inc(q.matmul(p[7][:], c_selBC[:, (16 + n) * 128:(17 + n) * 128],
                                bcsb[:], start=True, stop=True))
            bc_tick[n] = sT.n

            @block.scalar
            def _(q, n=n):
                q.wait_ge(sT.sem, bc_tick[n])
                sA.inc(q.copy(cbs[:, (n % 2) * T:(n % 2 + 1) * T], p[7][:]))
            cbs_tick[n] = sA.n

        def emit_dbu(i):
            n, dc = divmod(i, KI)
            pB = p[5 + (n % 2)]

            @block.vector
            def _(q, i=i, dc=dc, pB=pB,
                  sg=scan_tick.get(i - 3, v_base),
                  tg=bc_tick[n]):
                q.wait_ge(sV.sem, sg)
                q.wait_ge(sT.sem, tg)
                sV.inc(q.tensor_mul(dBb[:, (i % 3) * T:(i % 3 + 1) * T],
                                    dub[:, dc * T:(dc + 1) * T], pB[:]))
            dbu_tick[i] = sV.n

        emit_bc(0)
        emit_dbu(0)
        for n in range(DST):
            for dc in range(KI):
                i = n * KI + dc
                sl = dc * DST + n

                @block.scalar
                def _(q, dc=dc, i=i, sl=sl, vg=scan_tick.get(i - 2, v_base)):
                    q.wait_ge(sV.sem, vg)
                    sA.inc(q.activation(dAb[:, (i % 2) * T:(i % 2 + 1) * T],
                                        deltab[:, dc * T:(dc + 1) * T], AF.Exp,
                                        scale=c_A[:, sl:sl + 1]))
                dA_tick[i] = sA.n

                if i + 1 < NIT:
                    if (i + 1) % KI == 0:
                        emit_bc((i + 1) // KI)
                    emit_dbu(i + 1)

                @block.vector
                def _(q, i=i, sl=sl,
                      dg=dbu_tick[i], ag=dA_tick[i],
                      pg=ymul_tick.get(i - 2, gp_base)):
                    q.wait_ge(sV.sem, dg)
                    q.wait_ge(sA.sem, ag)
                    q.wait_ge(sP.sem, pg)
                    sV.inc(q.tensor_tensor_scan(zb[:, (i % 2) * T:(i % 2 + 1) * T],
                                                dAb[:, (i % 2) * T:(i % 2 + 1) * T],
                                                dBb[:, (i % 3) * T:(i % 3 + 1) * T],
                                                sstate[:, sl:sl + 1],
                                                ALU.mult, ALU.add))
                scan_tick[i] = sV.n

                @block.gpsimd
                def _(q, i=i, n=n, sl=sl,
                      sg=scan_tick[i],
                      yg=yadd_tick.get(i - 2, gp_base)):
                    q.wait_ge(sV.sem, sg)
                    q.wait_ge(sP.sem, yg)
                    q.tensor_copy(sstate[:, sl:sl + 1],
                                  zb[:, (i % 2) * T + T - 1:(i % 2) * T + T])
                    sP.inc(q.tensor_mul(gty[:, (i % 2) * T:(i % 2 + 1) * T],
                                        zb[:, (i % 2) * T:(i % 2 + 1) * T],
                                        cbs[:, (n % 2) * T:(n % 2 + 1) * T]))
                ymul_tick[i] = sP.n

                if i > 0:
                    pdc = (i - 1) % KI

                    @block.gpsimd
                    def _(q, i=i, pdc=pdc, mg=ymul_tick[i - 1]):
                        q.wait_ge(sP.sem, mg)
                        sP.inc(q.tensor_add(yb[:, pdc * T:(pdc + 1) * T],
                                            yb[:, pdc * T:(pdc + 1) * T],
                                            gty[:, ((i - 1) % 2) * T:((i - 1) % 2 + 1) * T]))
                    yadd_tick[i - 1] = sP.n

        @block.gpsimd
        def _(q, mg=ymul_tick[NIT - 1]):
            q.wait_ge(sP.sem, mg)
            sP.inc(q.tensor_add(yb[:, (KI - 1) * T:KI * T],
                                yb[:, (KI - 1) * T:KI * T],
                                gty[:, ((NIT - 1) % 2) * T:((NIT - 1) % 2 + 1) * T]))
        yadd_tick[NIT - 1] = sP.n

        for dc in range(KI):
            @block.vector
            def _(q, dc=dc):
                q.wait_ge(sP.sem, sP.n)
                q.drain()
                q.scalar_tensor_tensor(dub[:, dc * T:(dc + 1) * T],
                                       ub[:, dc * T:(dc + 1) * T],
                                       c_D[:, dc:dc + 1],
                                       yb[:, dc * T:(dc + 1) * T],
                                       ALU.mult, ALU.add)
                q.drain()
                sV.inc(q.tensor_mul(yb[:, dc * T:(dc + 1) * T],
                                    dub[:, dc * T:(dc + 1) * T],
                                    sresb[:, dc * T:(dc + 1) * T]))

        for g in range(2):
            ticks = {}
            base_tick = sT.n
            for k in range(KI):
                buf = k % 2

                sDb = sD0 if buf == 0 else sD1

                @block.sync
                def _(q, g=g, k=k, buf=buf, sDb=sDb, guard=(ticks.get(k - 2, base_tick) if k >= 2 else base_tick)):
                    q.wait_ge(sT.sem, guard)
                    sDb.inc(q.dma_start(
                        out=wslab[:, buf * 512:(buf + 1) * 512],
                        in_=w_out[k * 128:(k + 1) * 128, g * 512:(g + 1) * 512]), 16)

                dmadone = sDb.n

                @block.tensor
                def _(q, g=g, k=k, buf=buf, sDb=sDb, dmadone=dmadone):
                    q.wait_ge(sDb.sem, dmadone)
                    if k == 0:
                        q.wait_ge(sV.sem, sV.n)
                    for ml in range(4):
                        inst = q.matmul(p[ml][:],
                                        wslab[:, buf * 512 + ml * 128: buf * 512 + (ml + 1) * 128],
                                        yb[:, k * T:(k + 1) * T],
                                        start=(k == 0), stop=(k == KI - 1))
                    sT.inc(inst)

                ticks[k] = sT.n

            for ml in range(4):
                m = g * 4 + ml

                @block.vector
                def _(q, m=m, ml=ml):
                    q.wait_ge(sT.sem, sT.n)
                    q.wait_ge(sDF.sem, sDF.n)
                    q.drain()
                    q.tensor_mul(gt[:], t1b[:, m * T:(m + 1) * T], rs1s[:])
                    q.drain()
                    sV.inc(q.tensor_add(ab[:, m * T:(m + 1) * T], p[ml][:], gt[:]))

        @block.vector
        def _(q):
            q.drain()
            sV.inc(q.tensor_mul(sqb[:], ab[:], ab[:]))

        @block.tensor
        def _(q):
            q.wait_ge(sV.sem, sV.n)
            for k in range(KD):
                q.matmul(p[4][0:1, :], c_onesc[:], ab[:, k * T:(k + 1) * T],
                         start=(k == 0), stop=(k == KD - 1))
            for k in range(KD):
                inst = q.matmul(p[5][0:1, :], c_onesc[:], sqb[:, k * T:(k + 1) * T],
                                start=(k == 0), stop=(k == KD - 1))
            sT.inc(inst)

        @block.scalar
        def _(q):
            q.wait_ge(sT.sem, sT.n)
            q.activation(rowa[:], p[4][0:1, :], AF.Copy, scale=1.0 / D)
            q.activation(rowb[:], p[5][0:1, :], AF.Copy, scale=1.0 / D)
            q.drain()
            sA.inc(q.activation(rowc[:], rowa[:], AF.Square))

        @block.vector
        def _(q):
            q.wait_ge(sA.sem, sA.n)
            q.tensor_sub(rowv[:], rowb[:], rowc[:])
            q.drain()
            q.tensor_scalar_add(rowd[:], rowv[:], EPS)
            q.drain()
            sV.inc(q.reciprocal(rowe[:], rowd[:]))

        @block.scalar
        def _(q):
            q.wait_ge(sV.sem, sV.n)
            sA.inc(q.activation(rowb[:], rowe[:], AF.Sqrt))

        @block.vector
        def _(q):
            q.wait_ge(sA.sem, sA.n)
            q.tensor_mul(rowd[:], rowa[:], rowb[:])
            q.drain()
            sV.inc(q.tensor_sub(rowc[:], rowm[:], rowd[:]))

        @block.tensor
        def _(q):
            q.wait_ge(sV.sem, sV.n)
            q.matmul(p[6][:], c_onesr[:], rowb[:], start=True, stop=True)
            sT.inc(q.matmul(p[7][:], c_onesr[:], rowc[:], start=True, stop=True))

        @block.scalar
        def _(q):
            q.wait_ge(sT.sem, sT.n)
            q.copy(rsas[:], p[6][:])
            sA.inc(q.copy(r0s[:], p[7][:]))

        @block.vector
        def _(q):
            q.wait_ge(sA.sem, sA.n)
            for m in range(KD):
                q.tensor_mul(sqb[:, m * T:(m + 1) * T], ab[:, m * T:(m + 1) * T], rsas[:])
                q.drain()
                q.tensor_add(gt[:], sqb[:, m * T:(m + 1) * T],
                             t1b[:, m * T:(m + 1) * T])
                q.drain()
                inst = q.tensor_add(ab[:, m * T:(m + 1) * T], gt[:], r0s[:])
                q.drain()
            sV.inc(inst)

        @block.sync
        def _(q, t0=t0):
            q.wait_ge(sV.sem, sV.n)
            sDF.inc(q.dma_start(
                out=x1buf.rearrange("(m p) t -> p m t", p=128)[:, :, t0:t0 + T],
                in_=ab[:].rearrange("p (m t) -> p m t", t=T)), 16)

    for name in ("tensor", "vector", "scalar", "gpsimd", "sync"):
        def _barrier(q):
            for ctr in (sDC, sDX, sD0, sD1, sDF, sT, sV, sA, sP):
                q.wait_ge(ctr.sem, ctr.n)
        getattr(block, name)(_barrier)

    if DEBUG_TAPS:
        dbg_x1 = nc.dram_tensor("dbg_x1", [D, L], F32, kind="ExternalOutput").ap()
        dbg_cw = nc.dram_tensor("dbg_cw", [NE, 2048], F32, kind="ExternalOutput").ap()
        dbg_ln2 = nc.dram_tensor("dbg_ln2", [D, 2048], F32, kind="ExternalOutput").ap()
        dbg_lgt = nc.dram_tensor("dbg_lgt", [128, 8 * NE], F32, kind="ExternalOutput").ap()
        dbg_prb = nc.dram_tensor("dbg_prb", [128, 8 * NE], F32, kind="ExternalOutput").ap()
        dbg_rm = nc.dram_tensor("dbg_rm", [128, 16], F32, kind="ExternalOutput").ap()
        dbg_ffn = nc.dram_tensor("dbg_ffn", [D, 2048], F32, kind="ExternalOutput").ap()

        @block.gpsimd
        def _(q):
            sG.inc(q.dma_start(out=dbg_x1, in_=x1buf), 16)

    mctx.close()

    fctx = ExitStack()
    FSB = lambda name, shape, dt=F32: fctx.enter_context(nc.sbuf_tensor(name, list(shape), dt))
    FPS = lambda name, shape: fctx.enter_context(nc.psum_tensor(name, list(shape), F32))

    ln2f = FSB("ln2f", (128, KD * 1024))
    routedf = FSB("routedf", (128, KD * 1024))
    xsl = FSB("xsl", (128, KD * TF))
    hsb = FSB("hsb", (128, KH * TF))
    xtmp = hsb
    e1buf = FSB("e1buf", (128, KD * HD))
    e2buf = FSB("e2buf", (128, KH * D))
    w_rs = FSB("w_rs", (128, KD * NE))
    c_sel15 = FSB("c_sel15", (NE, NE * 128))
    c_eb2 = FSB("c_eb2", (NE, D))
    lgsb = FSB("lgsb", (NE, TF))
    cwf = FSB("cwf", (NE, 1024))
    lgT = FSB("lgT", (128, 4 * NE))
    prbuf = FSB("prbuf", (128, 4 * NE))
    cwt = FSB("cwt", (128, 4 * NE))
    qq = FSB("qq", (128, NE))
    rm4 = FSB("rm4", (128, 4))
    iv4 = FSB("iv4", (128, 4))
    top8 = FSB("top8", (128, 8))
    m8 = FSB("m8", (128, 8))
    frow0 = FSB("frow0", (1, TF))
    frow1 = FSB("frow1", (1, TF))
    frow2 = FSB("frow2", (1, TF))
    frow3 = FSB("frow3", (1, TF))
    frow4 = FSB("frow4", (1, TF))
    sv4 = FSB("sv4", (128, 4))
    prn = FSB("prn", (128, 4 * NE))
    fm = FSB("fm", (128, TF))
    frs = FSB("frs", (128, TF))
    fr0 = FSB("fr0", (128, TF))
    sqf = FSB("sqf", (128, TF))
    wfs = FSB("wfs", (128, 2 * 512))

    pf = [FPS(f"pf{i}", (128, TF)) for i in range(4)]
    pbc = FPS("pbc", (128, TF))
    prow0 = FPS("prow0", (1, TF))
    prow1 = FPS("prow1", (1, TF))
    pmisc = FPS("pmisc", (128, TF))

    @block.sync
    def _(q):
        sDC.inc(q.dma_start(out=w_rs[:].rearrange("p (kc m) -> p kc m", m=NE),
                            in_=w_r.rearrange("(kc p) m -> p kc m", p=128)), 16)
        sDC.inc(q.dma_start(out=c_sel15[:], in_=sel15), 16)
        sDC.inc(q.dma_start(out=c_eb2[:], in_=eb2_n), 16)

    for _name in ("tensor",):
        def _wconst2(q):
            q.wait_ge(sDC.sem, sDC.n)
        getattr(block, _name)(_wconst2)

    @block.vector
    def _(q):
        sV.inc(q.memset(m8[:], -1.0))

    def stream_grouped_matmul(wdram, KC, g, rhs_fn, extra_wait_v=True):
        ticks = {}
        base_tick = sT.n
        for k in range(KC):
            buf = k % 2

            sDb = sD0 if buf == 0 else sD1

            @block.sync
            def _(q, k=k, buf=buf, sDb=sDb, guard=(ticks.get(k - 2, base_tick) if k >= 2 else base_tick)):
                q.wait_ge(sT.sem, guard)
                sDb.inc(q.dma_start(
                    out=wfs[:, buf * 512:(buf + 1) * 512],
                    in_=wdram[k * 128:(k + 1) * 128, g * 512:(g + 1) * 512]), 16)

            dmadone = sDb.n

            @block.tensor
            def _(q, k=k, buf=buf, sDb=sDb, dmadone=dmadone):
                q.wait_ge(sDb.sem, dmadone)
                if k == 0:
                    if extra_wait_v:
                        q.wait_ge(sV.sem, sV.n)
                    q.wait_ge(sA.sem, sA.n)
                for ml in range(4):
                    inst = q.matmul(pf[ml][:],
                                    wfs[:, buf * 512 + ml * 128: buf * 512 + (ml + 1) * 128],
                                    rhs_fn(k),
                                    start=(k == 0), stop=(k == KC - 1))
                sT.inc(inst)

            ticks[k] = sT.n

    for th in range(2):
        tb0 = th * 1024

        for nt in range(NTT):
            c0 = tb0 + nt * TF

            @block.sync
            def _(q, c0=c0):
                q.wait_ge(sV.sem, sV.n)
                q.wait_ge(sA.sem, sA.n)
                q.wait_ge(sDF.sem, sDF.n)
                q.wait_ge(sG.sem, sG.n)
                sDL.inc(q.dma_start(
                    out=xsl[:].rearrange("p (m t) -> p m t", t=TF),
                    in_=x1buf.rearrange("(m p) t -> p m t", p=128)[:, :, c0:c0 + TF]), 16)
                sDL.inc(q.dma_start(
                    out=xtmp[:].rearrange("p (m t) -> p m t", t=TF),
                    in_=x1buf.rearrange("(m p) t -> p m t", p=128)[:, :, 2048 + c0:2048 + c0 + TF]), 16)

            @block.vector
            def _(q):
                q.wait_ge(sDL.sem, sDL.n)
                q.tensor_scalar_mul(xsl[:], xsl[:], c_svec[:, 0:1])
                q.drain()
                sV.inc(q.scalar_tensor_tensor(xsl[:], xtmp[:], c_svec[:, 1:2], xsl[:],
                                              ALU.mult, ALU.add))

            @block.sync
            def _(q, nt=nt, th=th):
                q.wait_ge(sV.sem, sV.n)
                sDF.inc(q.dma_start(
                    out=xselbuf.rearrange("(m p) t -> p m t", p=128)[:, :, th * 1024 + nt * TF: th * 1024 + (nt + 1) * TF],
                    in_=xsl[:].rearrange("p (m t) -> p m t", t=TF)), 16)

            for k in range(KD):
                @block.vector
                def _(q, k=k):
                    if k > 0:
                        q.wait_ge(sT.sem, sT.n)
                    q.drain()
                    sV.inc(q.tensor_mul(sqf[:], xsl[:, k * TF:(k + 1) * TF],
                                        xsl[:, k * TF:(k + 1) * TF]))

                @block.tensor
                def _(q, k=k):
                    q.wait_ge(sV.sem, sV.n)
                    q.matmul(prow0[:], c_onesc[:], xsl[:, k * TF:(k + 1) * TF],
                             start=(k == 0), stop=(k == KD - 1))
                    sT.inc(q.matmul(prow1[:], c_onesc[:], sqf[:],
                                    start=(k == 0), stop=(k == KD - 1)))

            @block.scalar
            def _(q):
                q.wait_ge(sT.sem, sT.n)
                q.activation(frow0[:], prow0[:], AF.Copy, scale=1.0 / D)
                q.activation(frow1[:], prow1[:], AF.Copy, scale=1.0 / D)
                q.drain()
                sA.inc(q.activation(frow2[:], frow0[:], AF.Square))

            @block.vector
            def _(q):
                q.wait_ge(sA.sem, sA.n)
                q.tensor_sub(frow3[:], frow1[:], frow2[:])
                q.drain()
                q.tensor_scalar_add(frow4[:], frow3[:], EPS)
                q.drain()
                sV.inc(q.reciprocal(frow3[:], frow4[:]))

            @block.scalar
            def _(q):
                q.wait_ge(sV.sem, sV.n)
                sA.inc(q.activation(frow1[:], frow3[:], AF.Sqrt))

            @block.tensor
            def _(q):
                q.wait_ge(sA.sem, sA.n)
                q.matmul(pbc[:], c_onesr[:], frow0[:], start=True, stop=True)
                sT.inc(q.matmul(pmisc[:], c_onesr[:], frow1[:], start=True, stop=True))

            @block.scalar
            def _(q):
                q.wait_ge(sT.sem, sT.n)
                q.copy(fm[:], pbc[:])
                sA.inc(q.copy(frs[:], pmisc[:]))

            @block.vector
            def _(q, nt=nt):
                q.wait_ge(sA.sem, sA.n)
                for k in range(KD):
                    q.tensor_sub(sqf[:], xsl[:, k * TF:(k + 1) * TF], fm[:])
                    q.drain()
                    inst = q.tensor_mul(
                        ln2f[:, k * 1024 + nt * TF: k * 1024 + (nt + 1) * TF],
                        sqf[:], frs[:])
                    q.drain()
                sV.inc(inst)

        if DEBUG_TAPS:
            @block.gpsimd
            def _(q, th=th):
                q.wait_ge(sV.sem, sV.n)
                sG.inc(q.dma_start(
                    out=dbg_ln2.rearrange("(m p) t -> p m t", p=128)[:, :, th * 1024:(th + 1) * 1024],
                    in_=ln2f[:].rearrange("p (m t) -> p m t", t=1024)), 16)

        for nt in range(NTT):
            @block.tensor
            def _(q, nt=nt):
                q.wait_ge(sV.sem, sV.n)
                q.wait_ge(sA.sem, sA.n)
                for k in range(KD):
                    inst = q.matmul(pmisc[0:NE, :], w_rs[:, k * NE:(k + 1) * NE],
                                    ln2f[:, k * 1024 + nt * TF: k * 1024 + (nt + 1) * TF],
                                    start=(k == 0), stop=(k == KD - 1))
                sT.inc(inst)

            @block.scalar
            def _(q):
                q.wait_ge(sT.sem, sT.n)
                sA.inc(q.activation(lgsb[:], pmisc[0:NE, :], AF.Identity, bias=c_rb[:, 0:1]))

            for j in range(4):
                @block.tensor
                def _(q, j=j):
                    q.wait_ge(sA.sem, sA.n)
                    sT.inc(q.transpose(pmisc[0:128, 0:NE], lgsb[:, j * 128:(j + 1) * 128],
                                       c_ident[0:NE, 0:NE]))

                @block.scalar
                def _(q, j=j):
                    q.wait_ge(sT.sem, sT.n)
                    sA.inc(q.copy(lgT[:, j * NE:(j + 1) * NE], pmisc[0:128, 0:NE]))

            @block.vector
            def _(q):
                q.wait_ge(sA.sem, sA.n)
                sV.inc(q.reduce_max(rm4[:], lgT[:].rearrange("p (j e) -> p j e", e=NE),
                                    axis=mybir.AxisListType.X, negate=True))

            for j in range(4):
                @block.scalar
                def _(q, j=j):
                    q.wait_ge(sV.sem, sV.n)
                    sA.inc(q.activation(prbuf[:, j * NE:(j + 1) * NE],
                                        lgT[:, j * NE:(j + 1) * NE], AF.Exp,
                                        bias=rm4[:, j:j + 1]))

            @block.vector
            def _(q):
                q.wait_ge(sA.sem, sA.n)
                q.reduce_sum(sv4[:], prbuf[:].rearrange("p (j e) -> p j e", e=NE),
                             axis=mybir.AxisListType.X)
                q.drain()
                q.reciprocal(iv4[:], sv4[:])
                q.drain()
                for j in range(4):
                    q.tensor_scalar_mul(prn[:, j * NE:(j + 1) * NE],
                                        prbuf[:, j * NE:(j + 1) * NE], iv4[:, j:j + 1])
                q.drain()
                for j in range(4):
                    q.max(top8[:], prn[:, j * NE:(j + 1) * NE])
                    q.drain()
                    q.tensor_copy(m8[:, 0:3], top8[:, 0:3])
                    q.drain()
                    q.match_replace(qq[:], m8[:], prn[:, j * NE:(j + 1) * NE], 0.0)
                    q.drain()
                    inst = q.tensor_sub(cwt[:, j * NE:(j + 1) * NE],
                                        prn[:, j * NE:(j + 1) * NE], qq[:])
                sV.inc(inst)

            if DEBUG_TAPS and nt == 0:
                @block.gpsimd
                def _(q, th=th):
                    q.wait_ge(sV.sem, sV.n)
                    sG.inc(q.dma_start(out=dbg_lgt[:, th * 4 * NE:(th + 1) * 4 * NE], in_=lgT[:]), 16)
                    sG.inc(q.dma_start(out=dbg_prb[:, th * 4 * NE:(th + 1) * 4 * NE], in_=prbuf[:]), 16)
                    sG.inc(q.dma_start(out=dbg_rm[:, th * 8:th * 8 + 4], in_=rm4[:]), 16)
                    sG.inc(q.dma_start(out=dbg_rm[:, th * 8 + 4:th * 8 + 8], in_=iv4[:]), 16)

            for j in range(4):
                @block.tensor
                def _(q, j=j):
                    q.wait_ge(sV.sem, sV.n)
                    q.wait_ge(sA.sem, sA.n)
                    q.wait_ge(sP.sem, sP.n)
                    sT.inc(q.transpose(pmisc[0:NE, 0:128], cwt[:, j * NE:(j + 1) * NE],
                                       c_ident[:]))

                @block.scalar
                def _(q, j=j, nt=nt):
                    q.wait_ge(sT.sem, sT.n)
                    sA.inc(q.copy(cwf[:, nt * TF + j * 128: nt * TF + (j + 1) * 128],
                                  pmisc[0:NE, 0:128]))

        for nt in range(NTT):
            for g in range(2):
                stream_grouped_matmul(w_s1, KD, g,
                                      lambda k, nt=nt: ln2f[:, k * 1024 + nt * TF: k * 1024 + (nt + 1) * TF])
                for ml in range(4):
                    m = g * 4 + ml

                    @block.scalar
                    def _(q, m=m, ml=ml):
                        q.wait_ge(sT.sem, sT.n)
                        sA.inc(q.activation(hsb[:, m * TF:(m + 1) * TF], pf[ml][:],
                                            AF.Relu, bias=c_sb1[:, m:m + 1]))
            for g in range(2):
                stream_grouped_matmul(w_s2, KH, g,
                                      lambda k: hsb[:, k * TF:(k + 1) * TF],
                                      extra_wait_v=False)
                for ml in range(4):
                    m = g * 4 + ml

                    @block.scalar
                    def _(q, m=m, ml=ml, nt=nt):
                        q.wait_ge(sT.sem, sT.n)
                        sA.inc(q.activation(
                            routedf[:, m * 1024 + nt * TF: m * 1024 + (nt + 1) * TF],
                            pf[ml][:], AF.Identity, bias=c_sb2[:, m:m + 1]))

        for e in range(NE):
            @block.sync
            def _(q, e=e):
                q.wait_ge(sT.sem, sT.n)
                sDW.inc(q.dma_start(
                    out=e1buf[:].rearrange("p (kc m) -> p kc m", m=HD),
                    in_=w_e1[e].rearrange("(kc p) m -> p kc m", p=128)), 16)
                sDW.inc(q.dma_start(
                    out=e2buf[:].rearrange("p (kc m) -> p kc m", m=D),
                    in_=w_e2[e].rearrange("(kc p) m -> p kc m", p=128)), 16)

            edma = sDW.n

            for nt in range(NTT):
                @block.tensor
                def _(q, e=e, nt=nt):
                    q.wait_ge(sV.sem, sV.n)
                    sT.inc(q.matmul(pbc[:], c_sel15[:, e * 128:(e + 1) * 128],
                                    cwf[:, nt * TF:(nt + 1) * TF], start=True, stop=True))

                for g in range(2):
                    @block.tensor
                    def _(q, e=e, nt=nt, g=g, edma=edma):
                        q.wait_ge(sDW.sem, edma)
                        q.wait_ge(sV.sem, sV.n)
                        q.wait_ge(sA.sem, sA.n)
                        for k in range(KD):
                            for ml in range(4):
                                inst = q.matmul(
                                    pf[ml][:],
                                    e1buf[:, k * HD + (g * 4 + ml) * 128: k * HD + (g * 4 + ml + 1) * 128],
                                    ln2f[:, k * 1024 + nt * TF: k * 1024 + (nt + 1) * TF],
                                    start=(k == 0), stop=(k == KD - 1))
                        sT.inc(inst)

                    for ml in range(4):
                        m = g * 4 + ml

                        @block.scalar
                        def _(q, e=e, m=m, ml=ml):
                            q.wait_ge(sT.sem, sT.n)
                            sA.inc(q.activation(hsb[:, m * TF:(m + 1) * TF], pf[ml][:],
                                                AF.Relu, bias=c_eb1[:, e * KH + m: e * KH + m + 1]))

                        @block.vector
                        def _(q, m=m):
                            q.wait_ge(sA.sem, sA.n)
                            sV.inc(q.tensor_mul(hsb[:, m * TF:(m + 1) * TF],
                                                hsb[:, m * TF:(m + 1) * TF], pbc[:]))

                for g in range(2):
                    @block.tensor
                    def _(q, e=e, nt=nt, g=g):
                        q.wait_ge(sV.sem, sV.n)
                        for ml in range(4):
                            m = g * 4 + ml
                            if e == 0:
                                q.matmul(pf[ml][:], c_eb2[:, m * 128:(m + 1) * 128],
                                         cwf[:, nt * TF:(nt + 1) * TF],
                                         start=True, stop=False)
                            for k in range(KH):
                                inst = q.matmul(
                                    pf[ml][:],
                                    e2buf[:, k * D + m * 128: k * D + (m + 1) * 128],
                                    hsb[:, k * TF:(k + 1) * TF],
                                    start=(k == 0 and e != 0), stop=(k == KH - 1))
                        sT.inc(inst)

                    for ml in range(4):
                        m = g * 4 + ml

                        @block.vector
                        def _(q, m=m, ml=ml, nt=nt):
                            q.wait_ge(sT.sem, sT.n)
                            sV.inc(q.scalar_tensor_tensor(
                                routedf[:, m * 1024 + nt * TF: m * 1024 + (nt + 1) * TF],
                                pf[ml][:], 0.0,
                                routedf[:, m * 1024 + nt * TF: m * 1024 + (nt + 1) * TF],
                                ALU.add, ALU.add))

        if DEBUG_TAPS:
            @block.gpsimd
            def _(q, th=th):
                q.wait_ge(sV.sem, sV.n)
                q.wait_ge(sA.sem, sA.n)
                sG.inc(q.dma_start(out=dbg_cw[:, th * 1024:(th + 1) * 1024], in_=cwf[:]), 16)
                sG.inc(q.dma_start(
                    out=dbg_ffn.rearrange("(m p) t -> p m t", p=128)[:, :, th * 1024:(th + 1) * 1024],
                    in_=routedf[:].rearrange("p (m t) -> p m t", t=1024)), 16)

        for nt in range(NTT):
            for k in range(KD):
                @block.vector
                def _(q, k=k, nt=nt):
                    if k > 0:
                        q.wait_ge(sT.sem, sT.n)
                    sV.inc(q.tensor_mul(sqf[:],
                                        routedf[:, k * 1024 + nt * TF: k * 1024 + (nt + 1) * TF],
                                        routedf[:, k * 1024 + nt * TF: k * 1024 + (nt + 1) * TF]))

                @block.tensor
                def _(q, k=k, nt=nt):
                    q.wait_ge(sV.sem, sV.n)
                    q.matmul(prow0[:], c_onesc[:],
                             routedf[:, k * 1024 + nt * TF: k * 1024 + (nt + 1) * TF],
                             start=(k == 0), stop=(k == KD - 1))
                    sT.inc(q.matmul(prow1[:], c_onesc[:], sqf[:],
                                    start=(k == 0), stop=(k == KD - 1)))

            @block.scalar
            def _(q):
                q.wait_ge(sT.sem, sT.n)
                q.activation(frow0[:], prow0[:], AF.Copy, scale=1.0 / D)
                q.activation(frow1[:], prow1[:], AF.Copy, scale=1.0 / D)
                q.drain()
                sA.inc(q.activation(frow2[:], frow0[:], AF.Square))

            @block.vector
            def _(q):
                q.wait_ge(sA.sem, sA.n)
                q.tensor_sub(frow3[:], frow1[:], frow2[:])
                q.drain()
                q.tensor_scalar_add(frow4[:], frow3[:], EPS)
                q.drain()
                sV.inc(q.reciprocal(frow3[:], frow4[:]))

            @block.scalar
            def _(q):
                q.wait_ge(sV.sem, sV.n)
                sA.inc(q.activation(frow1[:], frow3[:], AF.Sqrt))

            @block.vector
            def _(q):
                q.wait_ge(sA.sem, sA.n)
                sV.inc(q.scalar_tensor_tensor(frow2[:], frow0[:], -1.0, frow1[:],
                                              ALU.mult, ALU.mult))

            @block.tensor
            def _(q):
                q.wait_ge(sV.sem, sV.n)
                q.matmul(pbc[:], c_onesr[:], frow1[:], start=True, stop=True)
                sT.inc(q.matmul(pmisc[:], c_onesr[:], frow2[:], start=True, stop=True))

            @block.scalar
            def _(q):
                q.wait_ge(sT.sem, sT.n)
                q.copy(frs[:], pbc[:])
                sA.inc(q.copy(fr0[:], pmisc[:]))

            @block.sync
            def _(q, nt=nt, th=th):
                q.wait_ge(sA.sem, sA.n)
                q.wait_ge(sV.sem, sV.n)
                q.wait_ge(sDF.sem, sDF.n)
                sDL.inc(q.dma_start(
                    out=xsl[:].rearrange("p (m t) -> p m t", t=TF),
                    in_=xselbuf.rearrange("(m p) t -> p m t", p=128)[:, :, th * 1024 + nt * TF: th * 1024 + (nt + 1) * TF]), 16)

            @block.vector
            def _(q, nt=nt):
                q.wait_ge(sA.sem, sA.n)
                q.wait_ge(sDL.sem, sDL.n)
                q.wait_ge(sG.sem, sG.n)
                for m in range(KD):
                    q.tensor_mul(sqf[:],
                                 routedf[:, m * 1024 + nt * TF: m * 1024 + (nt + 1) * TF],
                                 frs[:])
                    q.drain()
                    q.tensor_add(fm[:], sqf[:], fr0[:])
                    q.drain()
                    inst = q.tensor_add(hsb[:, m * TF:(m + 1) * TF], fm[:],
                                        xsl[:, m * TF:(m + 1) * TF])
                    q.drain()
                sV.inc(inst)

            @block.gpsimd
            def _(q, nt=nt, th=th):
                q.wait_ge(sV.sem, sV.n)
                sG.inc(q.dma_start(
                    out=out.rearrange("(m p) t -> p m t", p=128)[:, :, th * 1024 + nt * TF: th * 1024 + (nt + 1) * TF],
                    in_=hsb[:, 0:KD * TF].rearrange("p (m t) -> p m t", t=TF)), 16)

    @block.gpsimd
    def _(q):
        q.wait_ge(sG.sem, sG.n)

    fctx.close()
    blk.__exit__(None, None, None)
    ctx.close()
    return nc


_NC_CACHE = None


def _prepare(inputs):
    f32 = lambda a: np.ascontiguousarray(np.asarray(a), dtype=np.float32)

    x = f32(inputs["L0_attn_x"])
    w_in = f32(np.asarray(inputs["L0_attn_in_proj_w"]).T)
    w_x = f32(np.asarray(inputs["L0_attn_x_proj_w"]).T)
    w_dt = f32(np.asarray(inputs["L0_attn_dt_proj_w"]).T)
    w_out = f32(np.asarray(inputs["L0_attn_out_proj_w"]).T)
    conv_w = f32(inputs["L0_attn_conv_w"])[:, 0, :]
    conv_wp = f32(conv_w.reshape(16, 128, 4).transpose(1, 0, 2).reshape(128, 64))
    conv_bp = f32(np.asarray(inputs["L0_attn_conv_b"]).reshape(16, 128).T)
    dtb_p = f32(np.asarray(inputs["L0_attn_dt_proj_b"]).reshape(16, 128).T)
    D_p = f32(np.asarray(inputs["L0_attn_D"]).reshape(16, 128).T)
    Alog = f32(inputs["L0_attn_A_log"])
    Alog_p = f32(Alog.reshape(16, 128, 16).transpose(1, 0, 2).reshape(128, 256))
    nw_p = f32(np.asarray(inputs["L0_attn_norm_w"]).reshape(8, 128).T)
    w_s1 = f32(inputs["L0_ffn_sW1"])
    w_s2 = f32(inputs["L0_ffn_sW2"])
    sb1_p = f32(np.asarray(inputs["L0_ffn_sb1"]).reshape(8, 128).T)
    sb2_p = f32(np.asarray(inputs["L0_ffn_sb2"]).reshape(8, 128).T)
    w_r = f32(inputs["L0_ffn_router_W"])
    rb_p = f32(np.asarray(inputs["L0_ffn_router_b"]).reshape(15, 1))
    w_e1 = f32(inputs["L0_ffn_eW1"])
    w_e2 = f32(inputs["L0_ffn_eW2"])
    eb1 = f32(inputs["L0_ffn_eb1"])
    eb1_p = f32(eb1.reshape(15, 8, 128).transpose(2, 0, 1).reshape(128, 120))
    eb2_n = f32(inputs["L0_ffn_eb2"])

    onesc = np.ones((128, 1), np.float32)
    onesr = np.ones((1, 128), np.float32)
    ident = np.eye(128, dtype=np.float32)
    selBC = np.zeros((32, 32 * 128), np.float32)
    for j in range(32):
        selBC[j, j * 128:(j + 1) * 128] = 1.0
    sel15 = np.zeros((15, 15 * 128), np.float32)
    for e in range(15):
        sel15[e, e * 128:(e + 1) * 128] = 1.0

    common = dict(
        w_in=w_in, w_x=w_x, w_dt=w_dt, w_out=w_out, conv_wp=conv_wp,
        conv_bp=conv_bp, dtb_p=dtb_p, D_p=D_p, Alog_p=Alog_p, nw_p=nw_p,
        w_s1=w_s1, w_s2=w_s2, sb1_p=sb1_p, sb2_p=sb2_p, w_r=w_r, rb_p=rb_p,
        w_e1=w_e1, w_e2=w_e2, eb1_p=eb1_p, eb2_n=eb2_n,
        onesc=onesc, onesr=onesr, ident=ident, selBC=selBC, sel15=sel15,
    )
    xTs = [f32(x[b].T) for b in range(B)]
    in_maps = []
    for c in range(8):
        b, h = c >> 1, c & 1
        svec = np.zeros((128, 2), np.float32)
        svec[:, 0] = 1.0 - h
        svec[:, 1] = float(h)
        in_maps.append(dict(common, xT=xTs[b], svec=svec))
    return in_maps


_NC_CACHE2 = None


def kernel(**inputs):
    global _NC_CACHE2
    if _NC_CACHE2 is None:
        _NC_CACHE2 = build_nc()
    nc = _NC_CACHE2
    in_maps = _prepare(inputs)
    res = run_bass_kernel_spmd(nc, in_maps, core_ids=list(range(8)))
    x2 = np.empty((B, L, D), np.float32)
    for c in range(8):
        b, h = c >> 1, c & 1
        x2[b, h * 2048:(h + 1) * 2048, :] = res.results[c]["out"].T
    return x2
